# version 10
# baseline (speedup 1.0000x reference)
import os
import numpy as np
from contextlib import ExitStack

import concourse.bass as bass
import concourse.mybir as mybir
from concourse.bass_utils import run_bass_kernel_spmd

F32 = mybir.dt.float32
BF16 = mybir.dt.bfloat16
AF = mybir.ActivationFunctionType
ALU = mybir.AluOpType
AX = mybir.AxisListType

P = 128
T = 2048
D = 1024
NC = 16
NDC = 8
NH = 16
HD = 64
NE = 8
EPS = 1e-5

DMA_SEMS = ("x0", "x1", "w0", "w1", "w2", "init", "initg", "wpm", "sm", "out")


def build_nc(debug=False, upto=5):
    ncrun = int(os.environ.get("K_NCRUN", str(NC)))
    skip = os.environ.get("K_SKIP", "")
    do_kv = "kv" not in skip
    do_gate = "gate" not in skip
    do_kproj = "kproj" not in skip
    do_vproj = "vproj" not in skip
    do_kevac = "kevac" not in skip
    do_vevac = "vevac" not in skip
    do_mk = "xmk" not in skip
    do_kt = "xkt" not in skip
    do_rk = "xrk" not in skip
    nc = bass.Bass("TRN2", target_bir_lowering=False,
                   detect_race_conditions=False)

    x_d = nc.dram_tensor("x", [T, D], F32, kind="ExternalInput").ap()
    wq_d = nc.dram_tensor("wq", [D, D], F32, kind="ExternalInput").ap()
    wk_d = nc.dram_tensor("wk", [D, D], F32, kind="ExternalInput").ap()
    wv_d = nc.dram_tensor("wv", [D, D], F32, kind="ExternalInput").ap()
    wp_d = nc.dram_tensor("wp", [D, D], F32, kind="ExternalInput").ap()
    bq_d = nc.dram_tensor("bq", [1, D], F32, kind="ExternalInput").ap()
    bk_d = nc.dram_tensor("bk", [1, D], F32, kind="ExternalInput").ap()
    bv_d = nc.dram_tensor("bv", [1, D], F32, kind="ExternalInput").ap()
    bp_d = nc.dram_tensor("bp", [1, D], F32, kind="ExternalInput").ap()
    gw_d = nc.dram_tensor("gw", [10, D], F32, kind="ExternalInput").ap()
    sm_d = nc.dram_tensor("smp", [65, D], F32, kind="ExternalInput").ap()
    gb_d = nc.dram_tensor("gb", [1, 10], F32, kind="ExternalInput").ap()
    ew_d = nc.dram_tensor("ew", [NE, D, D], F32, kind="ExternalInput").ap()
    eb_d = nc.dram_tensor("eb", [NE, D], F32, kind="ExternalInput").ap()
    y_d = nc.dram_tensor("y", [T, D], F32, kind="ExternalOutput").ap()
    dbg = {}
    if debug:
        dbg["xnT"] = nc.dram_tensor("dbg_xnT", [P, NDC * T], BF16, kind="ExternalOutput").ap()
        dbg["qT"] = nc.dram_tensor("dbg_qT", [P, NDC * T], BF16, kind="ExternalOutput").ap()
        dbg["vk"] = nc.dram_tensor("dbg_vk", [P, 512], BF16, kind="ExternalOutput").ap()
        dbg["c"] = nc.dram_tensor("dbg_c", [P, NC * NE], F32, kind="ExternalOutput").ap()
        dbg["M"] = nc.dram_tensor("dbg_M", [P, NDC * D], BF16, kind="ExternalOutput").ap()
        dbg["cb"] = nc.dram_tensor("dbg_cb", [9, T], BF16, kind="ExternalOutput").ap()
        dbg["ln"] = nc.dram_tensor("dbg_ln", [P, 16], F32, kind="ExternalOutput").ap()

    wq_r = wq_d.rearrange("(c p) g -> c p g", p=P)
    wk_r = wk_d.rearrange("(c p) g -> c p g", p=P)
    wv_r = wv_d.rearrange("(c p) g -> c p g", p=P)
    wp_r = wp_d.rearrange("(c p) g -> c p g", p=P)
    ew_r = ew_d.rearrange("e (c p) g -> e c p g", p=P)

    with ExitStack() as ctx:
        ent = ctx.enter_context

        xnT = ent(nc.sbuf_tensor("xnT", [P, NDC * T], BF16))[:]
        qT = ent(nc.sbuf_tensor("qT", [P, NDC * T], BF16))[:]
        M_sb = ent(nc.sbuf_tensor("M_sb", [P, NDC * D], BF16))[:]
        W_sb = [ent(nc.sbuf_tensor(f"W{b}", [P, NDC * D], BF16))[:] for b in range(3)]
        acc_h = ent(nc.sbuf_tensor("acc", [P, 8 * D], F32))
        acc = acc_h[:]
        acc_bf = acc_h[:].bitcast(BF16)
        x_pp = [ent(nc.sbuf_tensor(f"xpp{b}", [P, D], F32))[:] for b in range(2)]
        xn_bf = [ent(nc.sbuf_tensor(f"xnbf{b}", [P, D], BF16))[:] for b in range(2)]
        vk_sb = ent(nc.sbuf_tensor("vk_sb", [P, 512], BF16))[:]
        cb_sb = ent(nc.sbuf_tensor("cb_sb", [9, T], BF16))[:]
        c_all = ent(nc.sbuf_tensor("c_all", [P, NC * NE], F32))[:]
        c_bf = ent(nc.sbuf_tensor("c_bf", [P, NC * NE], BF16))[:]
        gwT = ent(nc.sbuf_tensor("gwT", [P, NDC * 10], BF16))[:]
        gw_bf = ent(nc.sbuf_tensor("gw_bf", [10, D], BF16))[:]
        bq8_bf = ent(nc.sbuf_tensor("bq8_bf", [NDC, P], BF16))[:]
        bqT = ent(nc.sbuf_tensor("bqT", [P, NDC], F32))[:]
        sm_sb = ent(nc.sbuf_tensor("sm_sb", [65, D], BF16))[:]
        gb_bf = ent(nc.sbuf_tensor("gb_bf", [1, 10], BF16))[:]
        eb_sb = sm_sb[0:9, :]
        bk_bf = sm_sb[32:33, :]
        bv_bf = sm_sb[64:65, :]
        ones = ent(nc.sbuf_tensor("ones", [65, T], BF16))[:]
        zero = ent(nc.sbuf_tensor("zero", [1, 512], BF16))[:]
        ident = ent(nc.sbuf_tensor("ident", [P, P], BF16))[:]
        cst_eps = ent(nc.sbuf_tensor("cst_eps", [P, 1], F32))[:]
        nc.const_aps.register(F32, EPS, cst_eps)
        sm = {}
        for nm in ("sum", "mean", "ssq", "ssqn", "nvar", "lnv", "rstd",
                   "m1", "nm1", "Z", "rz", "den", "ds", "rd"):
            sm[nm] = [ent(nc.sbuf_tensor(f"sm_{nm}{b}", [P, 1], F32))[:] for b in range(2)]
        egt = [ent(nc.sbuf_tensor(f"egt{b}", [P, 10], F32))[:] for b in range(2)]
        pt = [ent(nc.sbuf_tensor(f"pt{b}", [P, 10], F32))[:] for b in range(2)]
        m8t = [ent(nc.sbuf_tensor(f"m8t{b}", [P, 8], F32))[:] for b in range(2)]
        mskt = [ent(nc.sbuf_tensor(f"mskt{b}", [P, 8], F32))[:] for b in range(2)]
        wtst = [ent(nc.sbuf_tensor(f"wtst{b}", [P, 8], F32))[:] for b in range(2)]
        lndump = ent(nc.sbuf_tensor("lndump", [P, 16], F32))[:]

        sq_thr = acc[:, 0:D]
        xm_scr = acc[:, D:2 * D]
        def abf(c0, n):
            return acc_bf[:, c0:c0 + n]
        escr = [abf(4096 + 1024 * b, D) for b in range(2)]
        rscr = [abf(6144 + 1024 * b, D) for b in range(2)]
        mscr = [abf(8192 + 1024 * b, D) for b in range(2)]
        kscr = [abf(10240 + 1024 * b, D) for b in range(2)]
        vscr = [abf(12288 + 1024 * b, D) for b in range(2)]
        escrB = [abf(4096 + 2048 * b, 2048) for b in range(2)]
        rscrB = [abf(8192 + 2048 * b, 2048) for b in range(2)]
        mscrB = [abf(12288 + 2048 * b, 2048) for b in range(2)]

        P2a = ent(nc.psum_tensor("P2a", [P, D], F32))[:]
        P2b = ent(nc.psum_tensor("P2b", [P, D], F32))[:]
        P2c_h = ent(nc.psum_tensor("P2c", [P, D], F32))
        P2c = P2c_h[:]
        P2c_bf = P2c_h[:].bitcast(BF16)
        P1a = ent(nc.psum_tensor("P1a", [P, 512], F32))[:]
        P1b_h = ent(nc.psum_tensor("P1b", [P, 512], F32))
        P1b = P1b_h[:]
        P1b_bf = P1b_h[:].bitcast(BF16)

        sems = {}
        for nm in ("pe", "dve", "act", "pool") + DMA_SEMS:
            sems[nm] = ent(nc.semaphore(f"s_{nm}"))
        sems["dvex"] = ent(nc.semaphore("s_dvex"))
        sems["actx"] = ent(nc.semaphore("s_actx"))
        _cutc = {"dvex": 0, "actx": 0}

        def mkcut(semname):
            def cut(eng, inst):
                _cutc[semname] += 1
                inst.then_inc(sems[semname], 1)
                eng.wait_ge(sems[semname], _cutc[semname])
            return cut
        vcut = mkcut("dvex")
        acut = mkcut("actx")

        ops = {"pe": [], "dve": [], "act": [], "pool": [], "sync": []}
        cnt = {nm: 0 for nm in ("pe", "dve", "act", "pool") + DMA_SEMS}

        def emit(eng, fn, waits=(), inc=None):
            m = None
            if inc is not None:
                cnt[inc] += 16 if inc in DMA_SEMS else 1
                m = (inc, cnt[inc])
            ops[eng].append((tuple(waits), fn, m))
            return m

        MM = dict(skip_group_check=True)

        def f_init_pool0(g):
            g.memset(ones, 1.0)
            g.memset(zero, 0.0)
            g.memset(cst_eps, EPS)
            return g.memset(ident, 0.0)
        ms_initp0 = emit("pool", f_init_pool0, inc="pool")

        def f_init_pool(g):
            return g.affine_select(
                out=ident, in_=ident, compare_op=ALU.not_equal, fill=1.0,
                base=0, pattern=[[-1, P]], channel_multiplier=1)
        ms_initp = emit("pool", f_init_pool, waits=[ms_initp0], inc="pool")

        wload_cnt = [0, 0, 0]

        def load_weight(dram_chunked, seq_idx, waits):
            b = seq_idx % 3
            dst = W_sb[b]
            for h in range(2):
                src = dram_chunked[4 * h:4 * h + 4].rearrange("c p g -> p c g")
                dv = dst[:, 4096 * h:4096 * h + 4096]
                m = emit("pool",
                         (lambda s_, d_: (lambda g: g.dma_start(out=d_, in_=s_)))(src, dv),
                         waits=waits if h == 0 else (),
                         inc=f"w{b}")
                waits = ()
            wload_cnt[b] += 2
            return (f"w{b}", cnt[f"w{b}"])

        ms_gw = emit("pool", lambda g: g.dma_start(out=gw_bf, in_=gw_d),
                     inc="initg")
        emit("pool", lambda g: g.dma_start(
            out=bq8_bf, in_=bq_d.rearrange("o (c p) -> (o c) p", p=P)),
            inc="initg")
        ms_initg2 = ("initg", cnt["initg"])
        ms_wk = load_weight(wk_r, 0, ())
        ms_sm = emit("pool", lambda g: g.dma_start(out=sm_sb, in_=sm_d),
                     inc="sm")
        emit("pool", lambda g: g.dma_start(out=gb_bf, in_=gb_d), inc="init")
        emit("pool", lambda g: g.dma_start(out=cb_sb[8:9, :], in_=ones[0:1, 0:T]),
             waits=[ms_initp0], inc="init")
        init_all = ("init", cnt["init"])
        ms_wv = load_weight(wv_r, 1, ())
        ms_wq = load_weight(wq_r, 2, ())
        for h in range(2):
            srch = wp_r[4 * h:4 * h + 4].rearrange("c p g -> p c g")
            emit("pool",
                 (lambda s_, d_: (lambda g: g.dma_start(out=d_, in_=s_)))(
                     srch, M_sb[:, 4096 * h:4096 * h + 4096]),
                 inc="wpm")
        ms_wpm = ("wpm", cnt["wpm"])

        def f_gwT(t):
            last = None
            for c in range(NDC):
                last = t.matmul(P1b_bf[:, 10 * c:10 * c + 10],
                                gw_bf[0:10, P * c:P * c + P], ident[0:10, 0:10],
                                is_transpose=True, start=True, stop=True, **MM)
            return last
        def f_bqT(t):
            return t.matmul(P1b_bf[:, 128:128 + NDC], bq8_bf, ident[0:NDC, 0:NDC],
                            is_transpose=True, start=True, stop=True, **MM)

        def emit_gwT():
            pe = emit("pe", f_gwT, waits=[ms_gw, ms_initp], inc="pe")
            pe2 = emit("pe", f_bqT, waits=[ms_initg2], inc="pe")
            emit("dve", lambda v: v.tensor_copy(bqT, P1b_bf[:, 128:128 + NDC]),
                 waits=[pe2], inc="dve")
            return emit("dve", lambda v: v.tensor_copy(gwT, P1b_bf[:, 0:80]),
                        waits=[pe], inc="dve")

        ms_xload = {}
        ms_xn = {}
        ms_sq = {}
        ms_trPE = {}
        ms_trEvac = {}
        ms_kpe = {}
        ms_vpe = {}
        ms_gpe = {}
        ms_ek = {}
        ms_rk = {}
        ms_mk = {}
        ms_kt = {}
        ms_vcp = {}
        ms_kvpe = {}
        ms_gd1 = {}
        ms_eg = {}
        ms_gd2 = {}

        def xnT_sl(d, ii):
            return xnT[:, T * d + P * ii: T * d + P * ii + P]

        def e_front(i):
            b = i % 2
            w = []
            if i >= 2:
                w = [ms_xn[i - 2], ms_sq[i - 2]]
            ms_xload[i] = emit(
                "sync",
                (lambda ii, bb: (lambda s: s.dma_start(
                    out=x_pp[bb], in_=x_d[P * ii:P * ii + P, :])))(i, b),
                waits=w, inc=f"x{b}")

            def f_ln1(v, bb=b):
                vcut(v, v.reduce_sum(sm["sum"][bb], x_pp[bb], AX.X))
                return v.tensor_scalar_mul(sm["mean"][bb], sm["sum"][bb], 1.0 / D)
            emit("dve", f_ln1, waits=[ms_xload[i]])

            ms_sq[i] = emit(
                "act",
                (lambda bb: (lambda s: s.activation(
                    sq_thr, x_pp[bb], AF.Square, accum_out=sm["ssq"][bb])))(b),
                waits=[ms_xload[i]], inc="act")

            def f_ln2(v, bb=b):
                vcut(v, v.tensor_scalar_mul(sm["ssqn"][bb], sm["ssq"][bb], 1.0 / D))
                return v.scalar_tensor_tensor(
                    sm["nvar"][bb], sm["mean"][bb], sm["mean"][bb], sm["ssqn"][bb],
                    ALU.mult, ALU.subtract)
            ms_nvar = emit("dve", f_ln2, waits=[ms_sq[i]], inc="dve")

            def f_rstd(s, bb=b):
                acut(s, s.activation(sm["lnv"][bb], sm["nvar"][bb], AF.Ln,
                                     scale=-1.0, bias=EPS))
                return s.activation(sm["rstd"][bb], sm["lnv"][bb], AF.Exp,
                                    scale=-0.5)
            w = [ms_nvar]
            if i == 0:
                w.append(ms_initp)
            ms_rstd = emit("act", f_rstd, waits=w, inc="act")

            def f_norm(v, bb=b):
                v.tensor_scalar_sub(xm_scr, x_pp[bb], sm["mean"][bb])
                return v.tensor_scalar_mul(xn_bf[bb], xm_scr, sm["rstd"][bb])
            w = [ms_rstd]
            if i >= 2:
                w.append(ms_trPE[i - 2])
            ms_xn[i] = emit("dve", f_norm, waits=w, inc="dve")

        def e_tr(i):
            b = i % 2
            tr_bank = P2c_bf[:, 1024 * b:1024 * b + 1024]

            def f_tr(t, bb=b, trb=tr_bank):
                last = None
                for c in range(NDC):
                    last = t.matmul(trb[:, P * c:P * c + P],
                                    xn_bf[bb][:, P * c:P * c + P], ident,
                                    is_transpose=True, start=True, stop=True, **MM)
                return last
            w = [ms_xn[i]]
            if i >= 2:
                w.append(ms_trEvac[i - 2])
            ms_trPE[i] = emit("pe", f_tr, waits=w, inc="pe")

            xnT_dst = xnT.rearrange("p (d t) -> p d t", t=T)[:, :, P * i:P * i + P]

            def f_trev(v, trb=tr_bank, dst=xnT_dst):
                src_ = trb.rearrange("p (d t) -> p d t", t=P)
                return v.tensor_copy(dst, src_)
            ms_trEvac[i] = emit("dve", f_trev, waits=[ms_trPE[i]], inc="dve")

        def e_kv(i):
            b = i % 2

            def f_kv(t, ii=i, bb=b):
                last = None
                if ii == 0:
                    t.matmul(P1a, ones[0:1, 0:P], zero[0:1, 0:512],
                             start=True, stop=False, **MM)
                for h in range(NH):
                    pr = h // 2
                    par = h % 2
                    out = P1a[64 * par:64 * par + 64, 64 * pr:64 * pr + 64]
                    last = t.matmul(out, vscr[bb][:, HD * h:HD * h + HD],
                                    kscr[bb][:, HD * h:HD * h + HD],
                                    start=False,
                                    stop=(ii == ncrun - 1 and h == NH - 1), **MM)
                return last
            ms_kvpe[i] = emit("pe", f_kv, waits=[ms_kt[i], ms_vcp[i]], inc="pe")

        def e_mms(i):
            b = i % 2

            def f_g(t, ii=i):
                t.matmul(P1b[:, 0:10], ones[0:1, 0:P], gb_bf[0:1, 0:10],
                         start=True, stop=False, **MM)
                last = None
                for d in range(NDC):
                    last = t.matmul(P1b[:, 0:10], xnT_sl(d, ii),
                                    gwT[:, 10 * d:10 * d + 10],
                                    start=False, stop=d == NDC - 1, **MM)
                return last

            def f_k(t, ii=i):
                t.matmul(P2a[:, 0:512], ones[32:33, 0:P], bk_bf[0:1, 0:512],
                         start=True, stop=False, **MM)
                t.matmul(P2a[:, 512:1024], ones[32:33, 0:P], bk_bf[0:1, 512:1024],
                         start=True, stop=False, **MM)
                last = None
                for d in range(NDC):
                    st = d == NDC - 1
                    t.matmul(P2a[:, 0:512], xnT_sl(d, ii),
                             W_sb[0][:, D * d:D * d + 512],
                             start=False, stop=st, **MM)
                    last = t.matmul(P2a[:, 512:1024], xnT_sl(d, ii),
                                    W_sb[0][:, D * d + 512:D * d + 1024],
                                    start=False, stop=st, **MM)
                return last

            def f_v(t, ii=i):
                t.matmul(P2b[:, 0:512], ones[64:65, 0:P], bv_bf[0:1, 0:512],
                         start=True, stop=False, **MM)
                t.matmul(P2b[:, 512:1024], ones[64:65, 0:P], bv_bf[0:1, 512:1024],
                         start=True, stop=False, **MM)
                last = None
                for d in range(NDC):
                    st = d == NDC - 1
                    t.matmul(P2b[:, 0:512], xnT_sl(d, ii),
                             W_sb[1][:, D * d:D * d + 512],
                             start=False, stop=st, **MM)
                    last = t.matmul(P2b[:, 512:1024], xnT_sl(d, ii),
                                    W_sb[1][:, D * d + 512:D * d + 1024],
                                    start=False, stop=st, **MM)
                return last

            if do_kproj:
                w = [ms_trEvac[i]]
                if i == 0:
                    w += [ms_wk, ms_sm, init_all]
                else:
                    w += [ms_rk[i - 1]]
                ms_kpe[i] = emit("pe", f_k, waits=w, inc="pe")
            if do_vproj:
                w = [ms_trEvac[i]]
                if i == 0:
                    w.append(ms_wv)
                else:
                    w.append(ms_vcp[i - 1])
                ms_vpe[i] = emit("pe", f_v, waits=w, inc="pe")
            if do_gate:
                w = [ms_trEvac[i]]
                if i == 0:
                    w.append(ms_gwT)
                else:
                    w += [ms_eg[i - 1], ms_gd1[i - 1]]
                ms_gpe[i] = emit("pe", f_g, waits=w, inc="pe")

        def e_evacs(i):
            b = i % 2
            if not (do_kproj and do_kevac):
                return
            ms_ek[i] = emit(
                "act",
                (lambda bb: (lambda s: s.activation(escr[bb], P2a, AF.Exp)))(b),
                waits=[ms_kpe[i]] + ([ms_mk[i - 2]] if i >= 2 else []),
                inc="act")
            if not do_rk:
                return
            ms_rk[i] = emit(
                "act",
                (lambda bb: (lambda s: s.activation(rscr[bb], P2a, AF.Relu)))(b),
                waits=[ms_kpe[i]], inc="act")
            if not do_mk:
                return
            ms_mk[i] = emit(
                "dve",
                (lambda bb: (lambda v: v.tensor_scalar_min(mscr[bb], escr[bb], 1.0)))(b),
                waits=[ms_ek[i]] + ([ms_kt[i - 2]] if i >= 2 else []),
                inc="dve")
            if not do_kt:
                return
            w = [ms_mk[i], ms_rk[i]]
            if i >= 2 and do_kv:
                w.append(ms_kvpe[i - 2])
            ms_kt[i] = emit(
                "dve",
                (lambda bb: (lambda v: v.tensor_tensor(
                    kscr[bb], rscr[bb], mscr[bb], ALU.add)))(b),
                waits=w, inc="dve")
            if not (do_vproj and do_vevac):
                return
            w = [ms_vpe[i]]
            if i >= 2 and do_kv:
                w.append(ms_kvpe[i - 2])
            ms_vcp[i] = emit(
                "act",
                (lambda bb: (lambda s: s.copy(vscr[bb], P2b)))(b),
                waits=w, inc="act")

        def e_gate(i):
            if not do_gate:
                return
            b = i % 2

            def f_gd1(v, bb=b):
                vcut(v, v.reduce_max(sm["m1"][bb], P1b[:, 0:10], AX.X))
                return v.tensor_scalar_mul(sm["nm1"][bb], sm["m1"][bb], -1.0)
            ms_gd1[i] = emit("dve", f_gd1, waits=[ms_gpe[i]], inc="dve")

            w = [ms_gd1[i]]
            if i >= 2:
                w.append(ms_gd2[i - 2])
            ms_eg[i] = emit(
                "act",
                (lambda bb: (lambda s: s.activation(
                    egt[bb], P1b[:, 0:10], AF.Exp, bias=sm["nm1"][bb])))(b),
                waits=w, inc="act")

            def f_gd2(v, bb=b, ii=i):
                vcut(v, v.reduce_sum(sm["Z"][bb], egt[bb], AX.X))
                vcut(v, v.reciprocal(sm["rz"][bb], sm["Z"][bb]))
                vcut(v, v.tensor_scalar_mul(pt[bb], egt[bb], sm["rz"][bb]))
                vcut(v, v.max(m8t[bb], pt[bb]))
                vcut(v, v.tensor_single_scalar(mskt[bb], pt[bb][:, 0:8],
                                               m8t[bb][:, 1:2], ALU.is_ge))
                vcut(v, v.tensor_tensor(wtst[bb], mskt[bb], pt[bb][:, 0:8],
                                        ALU.mult))
                vcut(v, v.reduce_sum(sm["den"][bb], wtst[bb], AX.X))
                vcut(v, v.tensor_scalar_max(sm["ds"][bb], sm["den"][bb], 1e-30))
                vcut(v, v.reciprocal(sm["rd"][bb], sm["ds"][bb]))
                vcut(v, v.tensor_scalar_mul(c_all[:, 8 * ii:8 * ii + 8], wtst[bb],
                                            sm["rd"][bb]))
                return v.tensor_copy(c_bf[:, 8 * ii:8 * ii + 8],
                                     c_all[:, 8 * ii:8 * ii + 8])
            ms_gd2[i] = emit("dve", f_gd2, waits=[ms_eg[i]], inc="dve")

        e_front(0)
        if ncrun > 1:
            e_front(1)
        e_tr(0)
        ms_gwT = emit_gwT()
        for i in range(ncrun):
            if i + 2 < ncrun:
                e_front(i + 2)
            if i >= 1 and do_kv and (i - 1) in ms_kt and (i - 1) in ms_vcp:
                e_kv(i - 1)
            if i + 1 < ncrun:
                e_tr(i + 1)
            e_mms(i)
            e_evacs(i)
            e_gate(i)
        if do_kv and (ncrun - 1) in ms_kt and (ncrun - 1) in ms_vcp:
            e_kv(ncrun - 1)

        def f_ct(t):
            last = None
            for i in range(ncrun):
                last = t.matmul(P2c_bf[0:8, P * i:P * i + P],
                                c_bf[:, 8 * i:8 * i + 8], ident,
                                is_transpose=True, start=True, stop=True, **MM)
            return last
        if do_gate:
            ms_ctpe = emit("pe", f_ct,
                           waits=[ms_gd2[ncrun - 1], ms_trEvac[ncrun - 1],
                                  ms_trEvac[max(ncrun - 2, 0)]],
                           inc="pe")
            ms_cb = emit("dve", lambda v: v.tensor_copy(cb_sb[0:8, :], P2c_bf[0:8, :]),
                         waits=[ms_ctpe], inc="dve")
        if do_kv:
            ms_vk = emit("dve", lambda v: v.tensor_copy(vk_sb, P1a),
                         waits=[ms_kvpe[ncrun - 1]], inc="dve")

        ms_e = {}
        if upto >= 2:
            ms_e[0] = load_weight(ew_r[0], 3, waits=[ms_kpe[NC - 1]])
            ms_e[1] = load_weight(ew_r[1], 4, waits=[ms_vpe[NC - 1]])

        ms_Mpe = {}
        ms_Mev = {}
        for pr in range(NDC if upto >= 3 else 0):
            ps = P2a if pr % 2 == 0 else P2b

            def f_M(t, pp=pr, ps_=ps):
                last = None
                for par in range(2):
                    r0 = 64 * par
                    for hh in range(2):
                        last = t.matmul(
                            ps_[r0:r0 + 64, 512 * hh:512 * hh + 512],
                            vk_sb[r0:r0 + 64, 64 * pp:64 * pp + 64],
                            M_sb[r0:r0 + 64, D * pp + 512 * hh:D * pp + 512 * hh + 512],
                            start=True, stop=True, **MM)
                return last
            if pr == 0:
                w = [ms_vk, ms_wpm, ms_rk[ncrun - 1]]
            elif pr == 1:
                w = [ms_vcp[ncrun - 1]]
            else:
                w = [ms_Mev[pr - 2]]
            ms_Mpe[pr] = emit("pe", f_M, waits=w, inc="pe")
            ms_Mev[pr] = emit(
                "dve",
                (lambda pp, ps_: (lambda v: v.tensor_copy(
                    M_sb[:, D * pp:D * pp + D], ps_)))(pr, ps),
                waits=[ms_Mpe[pr]], inc="dve")


        ms_qpe = {}
        ms_eqa = {}
        ms_rqd = {}
        ms_mq = {}
        ms_qadd = {}
        for f in range(NDC if upto >= 2 else 0):
            fb = f % 2
            if fb == 0:
                wins = [P2a[:, 0:512], P2a[:, 512:1024],
                        P2b[:, 0:512], P2b[:, 512:1024]]
            else:
                wins = [P2c[:, 0:512], P2c[:, 512:1024], P1a, P1b]

            def f_q(t, ff=f, ws=wins):
                last = None
                for d in range(NDC):
                    for wdw in range(4):
                        last = t.matmul(
                            ws[wdw],
                            W_sb[2][:, D * d + P * ff:D * d + P * ff + P],
                            xnT[:, T * d + 512 * wdw:T * d + 512 * wdw + 512],
                            start=d == 0, stop=d == NDC - 1, **MM)
                return last
            if f == 0:
                w = [ms_wq, ms_Mev[NDC - 1]]
            elif f == 1:
                w = [ms_cb, ms_vk, ms_eg[NC - 1], ms_gd1[NC - 1]]
            else:
                w = [ms_rqd[f - 2]]
            ms_qpe[f] = emit("pe", f_q, waits=w, inc="pe")

            def f_eq(s, ws=wins, fb_=fb, ff=f):
                last = None
                for wdw in range(4):
                    last = s.activation(escrB[fb_][:, 512 * wdw:512 * wdw + 512],
                                        ws[wdw], AF.Exp, bias=bqT[:, ff:ff + 1])
                return last
            w = [ms_qpe[f]]
            if f >= 2:
                w.append(ms_mq[f - 2])
            if f in (0, 1):
                w += [ms_kvpe[NC - 1], ms_kt[NC - 1], ms_mk[NC - 1]]
            ms_eqa[f] = emit("act", f_eq, waits=w, inc="act")

            def f_rq(s, ws=wins, fb_=fb, ff=f):
                last = None
                for wdw in range(4):
                    last = s.activation(
                        rscrB[fb_][:, 512 * wdw:512 * wdw + 512], ws[wdw],
                        AF.Relu, bias=bqT[:, ff:ff + 1])
                return last
            w = [ms_qpe[f]]
            if f >= 2:
                w.append(ms_qadd[f - 2])
            if f in (0, 1):
                w += [ms_kvpe[NC - 1], ms_kt[NC - 1], ms_mk[NC - 1]]
            ms_rqd[f] = emit("act", f_rq, waits=w, inc="act")

            w = [ms_eqa[f]]
            if f >= 2:
                w.append(ms_qadd[f - 2])
            if f in (0, 1):
                w += [ms_kvpe[NC - 1], ms_kt[NC - 1]]
            ms_mq[f] = emit(
                "dve",
                (lambda fb_: (lambda v: v.tensor_scalar_min(
                    mscrB[fb_], escrB[fb_], 1.0)))(fb),
                waits=w, inc="dve")

            ms_qadd[f] = emit(
                "dve",
                (lambda fb_, ff: (lambda v: v.tensor_tensor(
                    qT[:, T * ff:T * ff + T], rscrB[fb_], mscrB[fb_], ALU.add)))(fb, f),
                waits=[ms_mq[f], ms_rqd[f]], inc="dve")

        if upto >= 2:
            ms_e[2] = load_weight(ew_r[2], 5, waits=[ms_qpe[NDC - 1]])

        bankfree = ({0: ms_rqd[NDC - 2], 1: ms_rqd[NDC - 2]}
                    if upto >= 3 else {})
        ms_dxl = {}
        ms_dj0 = {}
        usage_done = {}
        if upto >= 2:
            usage_done[0] = ms_kpe[NC - 1]
            usage_done[1] = ms_vpe[NC - 1]
            usage_done[2] = ms_qpe[NDC - 1]

        for half in range(2 if upto >= 4 else 0):
            gis = list(range(8 * half, 8 * half + 8))
            jorder = [1, 0] + list(range(2, 9)) if upto >= 5 else [0]
            for j in jorder:
                if j == 0:
                    for il, gi in enumerate(gis):
                        b = gi % 2
                        if half == 0 and il < 2:
                            w = [ms_xn[14 + il], ms_sq[14 + il]]
                        else:
                            w = [ms_dj0[gi - 2]]
                        ms_dxl[gi] = emit(
                            "sync",
                            (lambda g_, b_: (lambda s: s.dma_start(
                                out=x_pp[b_], in_=x_d[P * g_:P * g_ + P, :])))(gi, b),
                            waits=w, inc=f"x{b}")

                        ps = P2a if il % 2 == 0 else P2b

                        def f_j0(t, g_=gi, ps_=ps):
                            t.matmul(ps_[:, 0:512], cb_sb[0:9, P * g_:P * g_ + P],
                                     eb_sb[0:9, 0:512], start=True, stop=False, **MM)
                            t.matmul(ps_[:, 512:1024], cb_sb[0:9, P * g_:P * g_ + P],
                                     eb_sb[0:9, 512:1024], start=True, stop=False, **MM)
                            last = None
                            for d in range(NDC):
                                st = d == NDC - 1
                                t.matmul(ps_[:, 0:512],
                                         qT[:, T * d + P * g_:T * d + P * g_ + P],
                                         M_sb[:, D * d:D * d + 512],
                                         start=False, stop=st, **MM)
                                last = t.matmul(ps_[:, 512:1024],
                                                qT[:, T * d + P * g_:T * d + P * g_ + P],
                                                M_sb[:, D * d + 512:D * d + 1024],
                                                start=False, stop=st, **MM)
                            return last
                        w = [bankfree[il % 2]]
                        if half == 0 and il == 0:
                            w += [ms_qadd[NDC - 1], ms_Mev[NDC - 1], ms_cb]
                        mpe = emit("pe", f_j0, waits=w, inc="pe")

                        def f_acc0(v, g_=gi, il_=il, ps_=ps, b_=b):
                            v.scalar_tensor_tensor(
                                acc[:, D * il_:D * il_ + D], ps_, 1.0,
                                acc[:, D * il_:D * il_ + D], ALU.mult, ALU.add)
                            return v.tensor_tensor(
                                acc[:, D * il_:D * il_ + D],
                                acc[:, D * il_:D * il_ + D], x_pp[b_], ALU.add)
                        ms_dj0[gi] = emit("dve", f_acc0, waits=[mpe, ms_dxl[gi]],
                                          inc="dve")
                        bankfree[il % 2] = ms_dj0[gi]
                    continue

                e = j - 1
                seq_idx = 3 + 8 * half + e
                buf = seq_idx % 3
                last_mpe = None
                for il, gi in enumerate(gis):
                    ps = P2a if il % 2 == 0 else P2b

                    def f_je(t, g_=gi, ps_=ps, bf_=buf):
                        last = None
                        for d in range(NDC):
                            st = d == NDC - 1
                            t.matmul(ps_[:, 0:512],
                                     xnT[:, T * d + P * g_:T * d + P * g_ + P],
                                     W_sb[bf_][:, D * d:D * d + 512],
                                     start=d == 0, stop=st, **MM)
                            last = t.matmul(ps_[:, 512:1024],
                                            xnT[:, T * d + P * g_:T * d + P * g_ + P],
                                            W_sb[bf_][:, D * d + 512:D * d + 1024],
                                            start=d == 0, stop=st, **MM)
                        return last
                    w = [bankfree[il % 2]]
                    if il == 0:
                        w.append(ms_e[seq_idx - 3])
                    last_mpe = emit("pe", f_je, waits=w, inc="pe")

                    if j == 1:
                        def f_acce(v, g_=gi, il_=il, ps_=ps, e_=e):
                            return v.tensor_single_scalar(
                                acc[:, D * il_:D * il_ + D], ps_,
                                c_all[:, 8 * g_ + e_:8 * g_ + e_ + 1], ALU.mult)
                        w2 = [last_mpe]
                        if il == 0:
                            w2.append(ms_rqd[NDC - 1])
                    else:
                        def f_acce(v, g_=gi, il_=il, ps_=ps, e_=e):
                            return v.scalar_tensor_tensor(
                                acc[:, D * il_:D * il_ + D], ps_,
                                c_all[:, 8 * g_ + e_:8 * g_ + e_ + 1],
                                acc[:, D * il_:D * il_ + D], ALU.mult, ALU.add)
                        w2 = [last_mpe]
                    mev = emit("dve", f_acce, waits=w2, inc="dve")
                    bankfree[il % 2] = mev
                    if j == 8:
                        emit("sync",
                             (lambda g_, il_: (lambda s: s.dma_start(
                                 out=y_d[P * g_:P * g_ + P, :],
                                 in_=acc[:, D * il_:D * il_ + D])))(gi, il),
                             waits=[mev], inc="out")
                usage_done[seq_idx] = last_mpe
                nxt = seq_idx + 3
                if nxt <= 18:
                    ne_e = (nxt - 3) % 8
                    ms_e[nxt - 3] = load_weight(
                        ew_r[ne_e], nxt, waits=[usage_done[nxt - 3]])

        if debug:
            emit("sync", lambda s: s.dma_start(out=dbg["xnT"], in_=xnT),
                 waits=[ms_trEvac[ncrun - 1]], inc="out")
            if do_kv:
                emit("sync", lambda s: s.dma_start(out=dbg["vk"], in_=vk_sb),
                     waits=[ms_vk], inc="out")
            if do_gate:
                emit("sync", lambda s: s.dma_start(out=dbg["c"], in_=c_all),
                     waits=[ms_gd2[ncrun - 1]], inc="out")
                emit("sync", lambda s: s.dma_start(out=dbg["cb"], in_=cb_sb),
                     waits=[ms_cb, init_all], inc="out")
            if upto >= 2:
                emit("sync", lambda s: s.dma_start(out=dbg["qT"], in_=qT),
                     waits=[ms_qadd[NDC - 1]], inc="out")
            if upto >= 3:
                emit("sync", lambda s: s.dma_start(out=dbg["M"], in_=M_sb),
                     waits=[ms_Mev[NDC - 1]], inc="out")

        final_out = cnt["out"]
        emit("sync", None, waits=[("out", final_out)])

        block = ent(nc.Block())

        def runner(eng_name):
            def run(eng):
                for waits, fn, minc in ops[eng_name]:
                    for (snm, val) in waits:
                        eng.wait_ge(sems[snm], val)
                    if fn is None:
                        continue
                    inst = fn(eng)
                    if minc is not None:
                        snm, val = minc
                        step = 16 if snm in DMA_SEMS else 1
                        inst.then_inc(sems[snm], step)
            return run

        block.tensor(runner("pe"))
        block.vector(runner("dve"))
        block.scalar(runner("act"))
        block.gpsimd(runner("pool"))
        block.sync(runner("sync"))

    return nc


_NC_CACHE = {}


def get_nc(debug=False, upto=5):
    key = (debug, upto)
    if key not in _NC_CACHE:
        _NC_CACHE[key] = build_nc(debug, upto)
    return _NC_CACHE[key]


def make_in_maps(inputs):
    x = np.ascontiguousarray(np.asarray(inputs["L0_attn_x"], dtype=np.float32))
    common = {
        "wq": np.ascontiguousarray(np.asarray(inputs["L0_attn_Wq"], np.float32)),
        "wk": np.ascontiguousarray(np.asarray(inputs["L0_attn_Wk"], np.float32)),
        "wv": np.ascontiguousarray(np.asarray(inputs["L0_attn_Wv"], np.float32)),
        "wp": np.ascontiguousarray(np.asarray(inputs["L0_attn_Wp"], np.float32)),
        "bq": np.ascontiguousarray(np.asarray(inputs["L0_attn_bq"], np.float32).reshape(1, D)),
        "bk": np.ascontiguousarray(np.asarray(inputs["L0_attn_bk"], np.float32).reshape(1, D)),
        "bv": np.ascontiguousarray(np.asarray(inputs["L0_attn_bv"], np.float32).reshape(1, D)),
        "bp": np.ascontiguousarray(np.asarray(inputs["L0_attn_bp"], np.float32).reshape(1, D)),
        "gw": np.ascontiguousarray(np.asarray(inputs["L0_ffn_gate_w"], np.float32)),
        "smp": None,
        "gb": np.ascontiguousarray(np.asarray(inputs["L0_ffn_gate_b"], np.float32).reshape(1, 10)),
        "ew": np.ascontiguousarray(np.asarray(inputs["L0_ffn_expert_w"], np.float32)),
        "eb": np.ascontiguousarray(np.asarray(inputs["L0_ffn_expert_b"], np.float32)),
    }
    smp = np.zeros((65, D), np.float32)
    smp[0:8] = np.asarray(inputs["L0_ffn_expert_b"], np.float32)
    smp[8] = np.asarray(inputs["L0_attn_bp"], np.float32).reshape(D)
    smp[32] = np.asarray(inputs["L0_attn_bk"], np.float32).reshape(D)
    smp[64] = np.asarray(inputs["L0_attn_bv"], np.float32).reshape(D)
    common["smp"] = smp
    in_maps = []
    for c in range(8):
        m = dict(common)
        m["x"] = np.ascontiguousarray(x[c])
        in_maps.append(m)
    return in_maps


def kernel(**inputs):
    nc = get_nc(False)
    in_maps = make_in_maps(inputs)
    res = run_bass_kernel_spmd(nc, in_maps, core_ids=list(range(8)))
    out = np.stack([np.asarray(r["y"]) for r in res.results], axis=0)
    return out


# version 11
# speedup vs baseline: 1.0091x; 1.0091x over previous
import os
import numpy as np
from contextlib import ExitStack

import concourse.bass as bass
import concourse.mybir as mybir
from concourse.bass_utils import run_bass_kernel_spmd

F32 = mybir.dt.float32
BF16 = mybir.dt.bfloat16
AF = mybir.ActivationFunctionType
ALU = mybir.AluOpType
AX = mybir.AxisListType

P = 128
T = 2048
D = 1024
NC = 16
NDC = 8
NH = 16
HD = 64
NE = 8
EPS = 1e-5

DMA_SEMS = ("x0", "x1", "w0", "w1", "w2", "init", "initg", "wpm", "sm", "out")


def build_nc(debug=False, upto=5):
    ncrun = int(os.environ.get("K_NCRUN", str(NC)))
    skip = os.environ.get("K_SKIP", "")
    do_kv = "kv" not in skip
    do_gate = "gate" not in skip
    do_kproj = "kproj" not in skip
    do_vproj = "vproj" not in skip
    do_kevac = "kevac" not in skip
    do_vevac = "vevac" not in skip
    do_mk = "xmk" not in skip
    do_kt = "xkt" not in skip
    do_rk = "xrk" not in skip
    nc = bass.Bass("TRN2", target_bir_lowering=False,
                   detect_race_conditions=False)

    x_d = nc.dram_tensor("x", [T, D], F32, kind="ExternalInput").ap()
    wq_d = nc.dram_tensor("wq", [D, D], F32, kind="ExternalInput").ap()
    wk_d = nc.dram_tensor("wk", [D, D], F32, kind="ExternalInput").ap()
    wv_d = nc.dram_tensor("wv", [D, D], F32, kind="ExternalInput").ap()
    wp_d = nc.dram_tensor("wp", [D, D], F32, kind="ExternalInput").ap()
    bq_d = nc.dram_tensor("bq", [1, D], F32, kind="ExternalInput").ap()
    bk_d = nc.dram_tensor("bk", [1, D], F32, kind="ExternalInput").ap()
    bv_d = nc.dram_tensor("bv", [1, D], F32, kind="ExternalInput").ap()
    bp_d = nc.dram_tensor("bp", [1, D], F32, kind="ExternalInput").ap()
    gw_d = nc.dram_tensor("gw", [10, D], F32, kind="ExternalInput").ap()
    sm_d = nc.dram_tensor("smp", [65, D], F32, kind="ExternalInput").ap()
    gb_d = nc.dram_tensor("gb", [1, 10], F32, kind="ExternalInput").ap()
    ew_d = nc.dram_tensor("ew", [NE, D, D], F32, kind="ExternalInput").ap()
    eb_d = nc.dram_tensor("eb", [NE, D], F32, kind="ExternalInput").ap()
    y_d = nc.dram_tensor("y", [T, D], F32, kind="ExternalOutput").ap()
    dbg = {}
    if debug:
        dbg["xnT"] = nc.dram_tensor("dbg_xnT", [P, NDC * T], BF16, kind="ExternalOutput").ap()
        dbg["qT"] = nc.dram_tensor("dbg_qT", [P, NDC * T], BF16, kind="ExternalOutput").ap()
        dbg["vk"] = nc.dram_tensor("dbg_vk", [P, 512], BF16, kind="ExternalOutput").ap()
        dbg["c"] = nc.dram_tensor("dbg_c", [P, NC * NE], F32, kind="ExternalOutput").ap()
        dbg["M"] = nc.dram_tensor("dbg_M", [P, NDC * D], BF16, kind="ExternalOutput").ap()
        dbg["cb"] = nc.dram_tensor("dbg_cb", [9, T], BF16, kind="ExternalOutput").ap()
        dbg["ln"] = nc.dram_tensor("dbg_ln", [P, 16], F32, kind="ExternalOutput").ap()

    wq_r = wq_d.rearrange("(c p) g -> c p g", p=P)
    wk_r = wk_d.rearrange("(c p) g -> c p g", p=P)
    wv_r = wv_d.rearrange("(c p) g -> c p g", p=P)
    wp_r = wp_d.rearrange("(c p) g -> c p g", p=P)
    ew_r = ew_d.rearrange("e (c p) g -> e c p g", p=P)

    with ExitStack() as ctx:
        ent = ctx.enter_context

        xnT = ent(nc.sbuf_tensor("xnT", [P, NDC * T], BF16))[:]
        qT = ent(nc.sbuf_tensor("qT", [P, NDC * T], BF16))[:]
        M_sb = ent(nc.sbuf_tensor("M_sb", [P, NDC * D], BF16))[:]
        W_sb = [ent(nc.sbuf_tensor(f"W{b}", [P, NDC * D], BF16))[:] for b in range(3)]
        acc_h = ent(nc.sbuf_tensor("acc", [P, 8 * D], F32))
        acc = acc_h[:]
        acc_bf = acc_h[:].bitcast(BF16)
        x_pp = [ent(nc.sbuf_tensor(f"xpp{b}", [P, D], F32))[:] for b in range(2)]
        xn_bf = [ent(nc.sbuf_tensor(f"xnbf{b}", [P, D], BF16))[:] for b in range(2)]
        vk_sb = ent(nc.sbuf_tensor("vk_sb", [P, 512], BF16))[:]
        cb_sb = ent(nc.sbuf_tensor("cb_sb", [9, T], BF16))[:]
        c_all = ent(nc.sbuf_tensor("c_all", [P, NC * NE], F32))[:]
        c_bf = ent(nc.sbuf_tensor("c_bf", [P, NC * NE], BF16))[:]
        gwT = ent(nc.sbuf_tensor("gwT", [P, NDC * 10], BF16))[:]
        gw_bf = ent(nc.sbuf_tensor("gw_bf", [10, D], BF16))[:]
        bq8_bf = ent(nc.sbuf_tensor("bq8_bf", [NDC, P], BF16))[:]
        bqT = ent(nc.sbuf_tensor("bqT", [P, NDC], F32))[:]
        sm_sb = ent(nc.sbuf_tensor("sm_sb", [65, D], BF16))[:]
        gb_bf = ent(nc.sbuf_tensor("gb_bf", [1, 10], BF16))[:]
        eb_sb = sm_sb[0:9, :]
        bk_bf = sm_sb[32:33, :]
        bv_bf = sm_sb[64:65, :]
        ones = ent(nc.sbuf_tensor("ones", [65, T], BF16))[:]
        zero = ent(nc.sbuf_tensor("zero", [1, 512], BF16))[:]
        ident = ent(nc.sbuf_tensor("ident", [P, P], BF16))[:]
        cst_eps = ent(nc.sbuf_tensor("cst_eps", [P, 1], F32))[:]
        nc.const_aps.register(F32, EPS, cst_eps)
        sm = {}
        for nm in ("sum", "mean", "ssq", "ssqn", "nvar", "lnv", "rstd",
                   "m1", "nm1", "Z", "rz", "den", "ds", "rd"):
            sm[nm] = [ent(nc.sbuf_tensor(f"sm_{nm}{b}", [P, 1], F32))[:] for b in range(2)]
        egt = [ent(nc.sbuf_tensor(f"egt{b}", [P, 10], F32))[:] for b in range(2)]
        pt = [ent(nc.sbuf_tensor(f"pt{b}", [P, 10], F32))[:] for b in range(2)]
        m8t = [ent(nc.sbuf_tensor(f"m8t{b}", [P, 8], F32))[:] for b in range(2)]
        mskt = [ent(nc.sbuf_tensor(f"mskt{b}", [P, 8], F32))[:] for b in range(2)]
        wtst = [ent(nc.sbuf_tensor(f"wtst{b}", [P, 8], F32))[:] for b in range(2)]
        lndump = ent(nc.sbuf_tensor("lndump", [P, 16], F32))[:]

        sq_thr = acc[:, 0:D]
        xm_scr = acc[:, D:2 * D]
        def abf(c0, n):
            return acc_bf[:, c0:c0 + n]
        escr = [abf(4096 + 1024 * b, D) for b in range(2)]
        rscr = [abf(6144 + 1024 * b, D) for b in range(2)]
        mscr = [abf(8192 + 1024 * b, D) for b in range(2)]
        kscr = [abf(10240 + 1024 * b, D) for b in range(2)]
        vscr = [abf(12288 + 1024 * b, D) for b in range(2)]
        escrB = [abf(4096 + 2048 * b, 2048) for b in range(2)]
        rscrB = [abf(8192 + 2048 * b, 2048) for b in range(2)]
        mscrB = [abf(12288 + 2048 * b, 2048) for b in range(2)]

        P2a = ent(nc.psum_tensor("P2a", [P, D], F32))[:]
        P2b = ent(nc.psum_tensor("P2b", [P, D], F32))[:]
        P2c_h = ent(nc.psum_tensor("P2c", [P, D], F32))
        P2c = P2c_h[:]
        P2c_bf = P2c_h[:].bitcast(BF16)
        P1a = ent(nc.psum_tensor("P1a", [P, 512], F32))[:]
        P1b_h = ent(nc.psum_tensor("P1b", [P, 512], F32))
        P1b = P1b_h[:]
        P1b_bf = P1b_h[:].bitcast(BF16)

        sems = {}
        for nm in ("pe", "dve", "act", "pool") + DMA_SEMS:
            sems[nm] = ent(nc.semaphore(f"s_{nm}"))
        sems["dvex"] = ent(nc.semaphore("s_dvex"))
        sems["actx"] = ent(nc.semaphore("s_actx"))
        _cutc = {"dvex": 0, "actx": 0}

        def mkcut(semname):
            def cut(eng, inst):
                _cutc[semname] += 1
                inst.then_inc(sems[semname], 1)
                eng.wait_ge(sems[semname], _cutc[semname])
            return cut
        vcut = mkcut("dvex")
        acut = mkcut("actx")

        ops = {"pe": [], "dve": [], "act": [], "pool": [], "sync": []}
        cnt = {nm: 0 for nm in ("pe", "dve", "act", "pool") + DMA_SEMS}

        def emit(eng, fn, waits=(), inc=None):
            m = None
            if inc is not None:
                cnt[inc] += 16 if inc in DMA_SEMS else 1
                m = (inc, cnt[inc])
            ops[eng].append((tuple(waits), fn, m))
            return m

        MM = dict(skip_group_check=True)

        def f_init_pool0(g):
            g.memset(ones, 1.0)
            g.memset(zero, 0.0)
            g.memset(cst_eps, EPS)
            return g.memset(ident, 0.0)
        ms_initp0 = emit("pool", f_init_pool0, inc="pool")

        def f_init_pool(g):
            return g.affine_select(
                out=ident, in_=ident, compare_op=ALU.not_equal, fill=1.0,
                base=0, pattern=[[-1, P]], channel_multiplier=1)
        ms_initp = emit("pool", f_init_pool, waits=[ms_initp0], inc="pool")

        wload_cnt = [0, 0, 0]

        def load_weight(dram_chunked, seq_idx, waits):
            b = seq_idx % 3
            dst = W_sb[b]
            for h in range(2):
                src = dram_chunked[4 * h:4 * h + 4].rearrange("c p g -> p c g")
                dv = dst[:, 4096 * h:4096 * h + 4096]
                m = emit("pool",
                         (lambda s_, d_: (lambda g: g.dma_start(out=d_, in_=s_)))(src, dv),
                         waits=waits if h == 0 else (),
                         inc=f"w{b}")
                waits = ()
            wload_cnt[b] += 2
            return (f"w{b}", cnt[f"w{b}"])

        ms_gw = emit("pool", lambda g: g.dma_start(out=gw_bf, in_=gw_d),
                     inc="initg")
        emit("pool", lambda g: g.dma_start(
            out=bq8_bf, in_=bq_d.rearrange("o (c p) -> (o c) p", p=P)),
            inc="initg")
        ms_initg2 = ("initg", cnt["initg"])
        ms_wk = load_weight(wk_r, 0, ())
        ms_wv = load_weight(wv_r, 1, ())
        ms_sm = emit("pool", lambda g: g.dma_start(out=sm_sb, in_=sm_d),
                     inc="sm")
        emit("pool", lambda g: g.dma_start(out=gb_bf, in_=gb_d), inc="init")
        emit("pool", lambda g: g.dma_start(out=cb_sb[8:9, :], in_=ones[0:1, 0:T]),
             waits=[ms_initp0], inc="init")
        init_all = ("init", cnt["init"])
        ms_wq = load_weight(wq_r, 2, ())
        for h in range(2):
            srch = wp_r[4 * h:4 * h + 4].rearrange("c p g -> p c g")
            emit("pool",
                 (lambda s_, d_: (lambda g: g.dma_start(out=d_, in_=s_)))(
                     srch, M_sb[:, 4096 * h:4096 * h + 4096]),
                 inc="wpm")
        ms_wpm = ("wpm", cnt["wpm"])

        def f_gwT(t):
            last = None
            for c in range(NDC):
                last = t.matmul(P1b_bf[:, 10 * c:10 * c + 10],
                                gw_bf[0:10, P * c:P * c + P], ident[0:10, 0:10],
                                is_transpose=True, start=True, stop=True, **MM)
            return last
        def f_bqT(t):
            return t.matmul(P1b_bf[:, 128:128 + NDC], bq8_bf, ident[0:NDC, 0:NDC],
                            is_transpose=True, start=True, stop=True, **MM)

        def emit_gwT():
            pe = emit("pe", f_gwT, waits=[ms_gw, ms_initp], inc="pe")
            pe2 = emit("pe", f_bqT, waits=[ms_initg2], inc="pe")
            emit("dve", lambda v: v.tensor_copy(bqT, P1b_bf[:, 128:128 + NDC]),
                 waits=[pe2], inc="dve")
            return emit("dve", lambda v: v.tensor_copy(gwT, P1b_bf[:, 0:80]),
                        waits=[pe], inc="dve")

        ms_xload = {}
        ms_xn = {}
        ms_sq = {}
        ms_trPE = {}
        ms_trEvac = {}
        ms_kpe = {}
        ms_vpe = {}
        ms_gpe = {}
        ms_ek = {}
        ms_rk = {}
        ms_mk = {}
        ms_kt = {}
        ms_vcp = {}
        ms_kvpe = {}
        ms_gd1 = {}
        ms_eg = {}
        ms_gd2 = {}

        def xnT_sl(d, ii):
            return xnT[:, T * d + P * ii: T * d + P * ii + P]

        def e_front(i):
            b = i % 2
            w = []
            if i >= 2:
                w = [ms_xn[i - 2], ms_sq[i - 2]]
            ms_xload[i] = emit(
                "sync",
                (lambda ii, bb: (lambda s: s.dma_start(
                    out=x_pp[bb], in_=x_d[P * ii:P * ii + P, :])))(i, b),
                waits=w, inc=f"x{b}")

            def f_ln1(v, bb=b):
                vcut(v, v.reduce_sum(sm["sum"][bb], x_pp[bb], AX.X))
                return v.tensor_scalar_mul(sm["mean"][bb], sm["sum"][bb], 1.0 / D)
            emit("dve", f_ln1, waits=[ms_xload[i]])

            ms_sq[i] = emit(
                "act",
                (lambda bb: (lambda s: s.activation(
                    sq_thr, x_pp[bb], AF.Square, accum_out=sm["ssq"][bb])))(b),
                waits=[ms_xload[i]], inc="act")

            def f_ln2(v, bb=b):
                vcut(v, v.tensor_scalar_mul(sm["ssqn"][bb], sm["ssq"][bb], 1.0 / D))
                return v.scalar_tensor_tensor(
                    sm["nvar"][bb], sm["mean"][bb], sm["mean"][bb], sm["ssqn"][bb],
                    ALU.mult, ALU.subtract)
            ms_nvar = emit("dve", f_ln2, waits=[ms_sq[i]], inc="dve")

            def f_rstd(s, bb=b):
                acut(s, s.activation(sm["lnv"][bb], sm["nvar"][bb], AF.Ln,
                                     scale=-1.0, bias=EPS))
                return s.activation(sm["rstd"][bb], sm["lnv"][bb], AF.Exp,
                                    scale=-0.5)
            w = [ms_nvar]
            if i == 0:
                w.append(ms_initp)
            ms_rstd = emit("act", f_rstd, waits=w, inc="act")

            def f_norm(v, bb=b):
                v.tensor_scalar_sub(xm_scr, x_pp[bb], sm["mean"][bb])
                return v.tensor_scalar_mul(xn_bf[bb], xm_scr, sm["rstd"][bb])
            w = [ms_rstd]
            if i >= 2:
                w.append(ms_trPE[i - 2])
            ms_xn[i] = emit("dve", f_norm, waits=w, inc="dve")

        def e_tr(i):
            b = i % 2
            tr_bank = P2c_bf[:, 1024 * b:1024 * b + 1024]

            def f_tr(t, bb=b, trb=tr_bank):
                last = None
                for c in range(NDC):
                    last = t.matmul(trb[:, P * c:P * c + P],
                                    xn_bf[bb][:, P * c:P * c + P], ident,
                                    is_transpose=True, start=True, stop=True, **MM)
                return last
            w = [ms_xn[i]]
            if i >= 2:
                w.append(ms_trEvac[i - 2])
            ms_trPE[i] = emit("pe", f_tr, waits=w, inc="pe")

            xnT_dst = xnT.rearrange("p (d t) -> p d t", t=T)[:, :, P * i:P * i + P]

            def f_trev(v, trb=tr_bank, dst=xnT_dst):
                src_ = trb.rearrange("p (d t) -> p d t", t=P)
                return v.tensor_copy(dst, src_)
            ms_trEvac[i] = emit("dve", f_trev, waits=[ms_trPE[i]], inc="dve")

        def e_kv(i):
            b = i % 2

            def f_kv(t, ii=i, bb=b):
                last = None
                if ii == 0:
                    t.matmul(P1a, ones[0:1, 0:P], zero[0:1, 0:512],
                             start=True, stop=False, **MM)
                for h in range(NH):
                    pr = h // 2
                    par = h % 2
                    out = P1a[64 * par:64 * par + 64, 64 * pr:64 * pr + 64]
                    last = t.matmul(out, vscr[bb][:, HD * h:HD * h + HD],
                                    kscr[bb][:, HD * h:HD * h + HD],
                                    start=False,
                                    stop=(ii == ncrun - 1 and h == NH - 1), **MM)
                return last
            ms_kvpe[i] = emit("pe", f_kv, waits=[ms_kt[i], ms_vcp[i]], inc="pe")

        def e_mms(i):
            b = i % 2

            def f_g(t, ii=i):
                t.matmul(P1b[:, 0:10], ones[0:1, 0:P], gb_bf[0:1, 0:10],
                         start=True, stop=False, **MM)
                last = None
                for d in range(NDC):
                    last = t.matmul(P1b[:, 0:10], xnT_sl(d, ii),
                                    gwT[:, 10 * d:10 * d + 10],
                                    start=False, stop=d == NDC - 1, **MM)
                return last

            def f_k(t, ii=i):
                last = None
                for d in range(NDC):
                    t.matmul(P2a[:, 0:512], xnT_sl(d, ii),
                             W_sb[0][:, D * d:D * d + 512],
                             start=d == 0, stop=False, **MM)
                    last = t.matmul(P2a[:, 512:1024], xnT_sl(d, ii),
                                    W_sb[0][:, D * d + 512:D * d + 1024],
                                    start=d == 0, stop=False, **MM)
                return last

            def f_kb(t):
                t.matmul(P2a[:, 0:512], ones[32:33, 0:P], bk_bf[0:1, 0:512],
                         start=False, stop=True, **MM)
                return t.matmul(P2a[:, 512:1024], ones[32:33, 0:P],
                                bk_bf[0:1, 512:1024],
                                start=False, stop=True, **MM)

            def f_v(t, ii=i):
                last = None
                for d in range(NDC):
                    t.matmul(P2b[:, 0:512], xnT_sl(d, ii),
                             W_sb[1][:, D * d:D * d + 512],
                             start=d == 0, stop=False, **MM)
                    last = t.matmul(P2b[:, 512:1024], xnT_sl(d, ii),
                                    W_sb[1][:, D * d + 512:D * d + 1024],
                                    start=d == 0, stop=False, **MM)
                return last

            def f_vb(t):
                t.matmul(P2b[:, 0:512], ones[64:65, 0:P], bv_bf[0:1, 0:512],
                         start=False, stop=True, **MM)
                return t.matmul(P2b[:, 512:1024], ones[64:65, 0:P],
                                bv_bf[0:1, 512:1024],
                                start=False, stop=True, **MM)

            if do_kproj:
                w = [ms_trEvac[i]]
                if i == 0:
                    w += [ms_wk]
                else:
                    w += [ms_rk[i - 1]]
                emit("pe", f_k, waits=w, inc="pe")
                ms_kpe[i] = emit("pe", f_kb,
                                 waits=[ms_sm] if i == 0 else (), inc="pe")
            if do_vproj:
                w = [ms_trEvac[i]]
                if i == 0:
                    w.append(ms_wv)
                else:
                    w.append(ms_vcp[i - 1])
                emit("pe", f_v, waits=w, inc="pe")
                ms_vpe[i] = emit("pe", f_vb,
                                 waits=[ms_sm] if i == 0 else (), inc="pe")
            if do_gate:
                w = [ms_trEvac[i]]
                if i == 0:
                    w += [ms_gwT, init_all]
                else:
                    w += [ms_eg[i - 1], ms_gd1[i - 1]]
                ms_gpe[i] = emit("pe", f_g, waits=w, inc="pe")

        def e_evacs(i):
            b = i % 2
            if not (do_kproj and do_kevac):
                return
            ms_ek[i] = emit(
                "act",
                (lambda bb: (lambda s: s.activation(escr[bb], P2a, AF.Exp)))(b),
                waits=[ms_kpe[i]] + ([ms_mk[i - 2]] if i >= 2 else []),
                inc="act")
            if not do_rk:
                return
            ms_rk[i] = emit(
                "act",
                (lambda bb: (lambda s: s.activation(rscr[bb], P2a, AF.Relu)))(b),
                waits=[ms_kpe[i]], inc="act")
            if not do_mk:
                return
            ms_mk[i] = emit(
                "dve",
                (lambda bb: (lambda v: v.tensor_scalar_min(mscr[bb], escr[bb], 1.0)))(b),
                waits=[ms_ek[i]] + ([ms_kt[i - 2]] if i >= 2 else []),
                inc="dve")
            if not do_kt:
                return
            w = [ms_mk[i], ms_rk[i]]
            if i >= 2 and do_kv:
                w.append(ms_kvpe[i - 2])
            ms_kt[i] = emit(
                "dve",
                (lambda bb: (lambda v: v.tensor_tensor(
                    kscr[bb], rscr[bb], mscr[bb], ALU.add)))(b),
                waits=w, inc="dve")
            if not (do_vproj and do_vevac):
                return
            w = [ms_vpe[i]]
            if i >= 2 and do_kv:
                w.append(ms_kvpe[i - 2])
            ms_vcp[i] = emit(
                "act",
                (lambda bb: (lambda s: s.copy(vscr[bb], P2b)))(b),
                waits=w, inc="act")

        def e_gate(i):
            if not do_gate:
                return
            b = i % 2

            def f_gd1(v, bb=b):
                vcut(v, v.reduce_max(sm["m1"][bb], P1b[:, 0:10], AX.X))
                return v.tensor_scalar_mul(sm["nm1"][bb], sm["m1"][bb], -1.0)
            ms_gd1[i] = emit("dve", f_gd1, waits=[ms_gpe[i]], inc="dve")

            w = [ms_gd1[i]]
            if i >= 2:
                w.append(ms_gd2[i - 2])
            ms_eg[i] = emit(
                "act",
                (lambda bb: (lambda s: s.activation(
                    egt[bb], P1b[:, 0:10], AF.Exp, bias=sm["nm1"][bb])))(b),
                waits=w, inc="act")

            def f_gd2(v, bb=b, ii=i):
                vcut(v, v.reduce_sum(sm["Z"][bb], egt[bb], AX.X))
                vcut(v, v.reciprocal(sm["rz"][bb], sm["Z"][bb]))
                vcut(v, v.tensor_scalar_mul(pt[bb], egt[bb], sm["rz"][bb]))
                vcut(v, v.max(m8t[bb], pt[bb]))
                vcut(v, v.tensor_single_scalar(mskt[bb], pt[bb][:, 0:8],
                                               m8t[bb][:, 1:2], ALU.is_ge))
                vcut(v, v.tensor_tensor(wtst[bb], mskt[bb], pt[bb][:, 0:8],
                                        ALU.mult))
                vcut(v, v.reduce_sum(sm["den"][bb], wtst[bb], AX.X))
                vcut(v, v.tensor_scalar_max(sm["ds"][bb], sm["den"][bb], 1e-30))
                vcut(v, v.reciprocal(sm["rd"][bb], sm["ds"][bb]))
                vcut(v, v.tensor_scalar_mul(c_all[:, 8 * ii:8 * ii + 8], wtst[bb],
                                            sm["rd"][bb]))
                return v.tensor_copy(c_bf[:, 8 * ii:8 * ii + 8],
                                     c_all[:, 8 * ii:8 * ii + 8])
            ms_gd2[i] = emit("dve", f_gd2, waits=[ms_eg[i]], inc="dve")

        e_front(0)
        if ncrun > 1:
            e_front(1)
        e_tr(0)
        ms_gwT = emit_gwT()
        for i in range(ncrun):
            if i + 2 < ncrun:
                e_front(i + 2)
            if i >= 1 and do_kv and (i - 1) in ms_kt and (i - 1) in ms_vcp:
                e_kv(i - 1)
            if i + 1 < ncrun:
                e_tr(i + 1)
            e_mms(i)
            e_evacs(i)
            e_gate(i)
        if do_kv and (ncrun - 1) in ms_kt and (ncrun - 1) in ms_vcp:
            e_kv(ncrun - 1)

        def f_ct(t):
            last = None
            for i in range(ncrun):
                last = t.matmul(P2c_bf[0:8, P * i:P * i + P],
                                c_bf[:, 8 * i:8 * i + 8], ident,
                                is_transpose=True, start=True, stop=True, **MM)
            return last
        if do_gate:
            ms_ctpe = emit("pe", f_ct,
                           waits=[ms_gd2[ncrun - 1], ms_trEvac[ncrun - 1],
                                  ms_trEvac[max(ncrun - 2, 0)]],
                           inc="pe")
            ms_cb = emit("dve", lambda v: v.tensor_copy(cb_sb[0:8, :], P2c_bf[0:8, :]),
                         waits=[ms_ctpe], inc="dve")
        if do_kv:
            ms_vk = emit("dve", lambda v: v.tensor_copy(vk_sb, P1a),
                         waits=[ms_kvpe[ncrun - 1]], inc="dve")

        ms_e = {}
        if upto >= 2:
            ms_e[0] = load_weight(ew_r[0], 3, waits=[ms_kpe[NC - 1]])
            ms_e[1] = load_weight(ew_r[1], 4, waits=[ms_vpe[NC - 1]])

        ms_Mpe = {}
        ms_Mev = {}
        for pr in range(NDC if upto >= 3 else 0):
            ps = P2a if pr % 2 == 0 else P2b

            def f_M(t, pp=pr, ps_=ps):
                last = None
                for par in range(2):
                    r0 = 64 * par
                    for hh in range(2):
                        last = t.matmul(
                            ps_[r0:r0 + 64, 512 * hh:512 * hh + 512],
                            vk_sb[r0:r0 + 64, 64 * pp:64 * pp + 64],
                            M_sb[r0:r0 + 64, D * pp + 512 * hh:D * pp + 512 * hh + 512],
                            start=True, stop=True, **MM)
                return last
            if pr == 0:
                w = [ms_vk, ms_wpm, ms_rk[ncrun - 1]]
            elif pr == 1:
                w = [ms_vcp[ncrun - 1]]
            else:
                w = [ms_Mev[pr - 2]]
            ms_Mpe[pr] = emit("pe", f_M, waits=w, inc="pe")
            ms_Mev[pr] = emit(
                "dve",
                (lambda pp, ps_: (lambda v: v.tensor_copy(
                    M_sb[:, D * pp:D * pp + D], ps_)))(pr, ps),
                waits=[ms_Mpe[pr]], inc="dve")


        ms_qpe = {}
        ms_eqa = {}
        ms_rqd = {}
        ms_mq = {}
        ms_qadd = {}
        for f in range(NDC if upto >= 2 else 0):
            fb = f % 2
            if fb == 0:
                wins = [P2a[:, 0:512], P2a[:, 512:1024],
                        P2b[:, 0:512], P2b[:, 512:1024]]
            else:
                wins = [P2c[:, 0:512], P2c[:, 512:1024], P1a, P1b]

            def f_q(t, ff=f, ws=wins):
                last = None
                for d in range(NDC):
                    for wdw in range(4):
                        last = t.matmul(
                            ws[wdw],
                            W_sb[2][:, D * d + P * ff:D * d + P * ff + P],
                            xnT[:, T * d + 512 * wdw:T * d + 512 * wdw + 512],
                            start=d == 0, stop=d == NDC - 1, **MM)
                return last
            if f == 0:
                w = [ms_wq, ms_Mev[NDC - 1]]
            elif f == 1:
                w = [ms_cb, ms_vk, ms_eg[NC - 1], ms_gd1[NC - 1]]
            else:
                w = [ms_rqd[f - 2]]
            ms_qpe[f] = emit("pe", f_q, waits=w, inc="pe")

            def f_eq(s, ws=wins, fb_=fb, ff=f):
                last = None
                for wdw in range(4):
                    last = s.activation(escrB[fb_][:, 512 * wdw:512 * wdw + 512],
                                        ws[wdw], AF.Exp, bias=bqT[:, ff:ff + 1])
                return last
            w = [ms_qpe[f]]
            if f >= 2:
                w.append(ms_mq[f - 2])
            if f in (0, 1):
                w += [ms_kvpe[NC - 1], ms_kt[NC - 1], ms_mk[NC - 1]]
            ms_eqa[f] = emit("act", f_eq, waits=w, inc="act")

            def f_rq(s, ws=wins, fb_=fb, ff=f):
                last = None
                for wdw in range(4):
                    last = s.activation(
                        rscrB[fb_][:, 512 * wdw:512 * wdw + 512], ws[wdw],
                        AF.Relu, bias=bqT[:, ff:ff + 1])
                return last
            w = [ms_qpe[f]]
            if f >= 2:
                w.append(ms_qadd[f - 2])
            if f in (0, 1):
                w += [ms_kvpe[NC - 1], ms_kt[NC - 1], ms_mk[NC - 1]]
            ms_rqd[f] = emit("act", f_rq, waits=w, inc="act")

            w = [ms_eqa[f]]
            if f >= 2:
                w.append(ms_qadd[f - 2])
            if f in (0, 1):
                w += [ms_kvpe[NC - 1], ms_kt[NC - 1]]
            ms_mq[f] = emit(
                "dve",
                (lambda fb_: (lambda v: v.tensor_scalar_min(
                    mscrB[fb_], escrB[fb_], 1.0)))(fb),
                waits=w, inc="dve")

            ms_qadd[f] = emit(
                "dve",
                (lambda fb_, ff: (lambda v: v.tensor_tensor(
                    qT[:, T * ff:T * ff + T], rscrB[fb_], mscrB[fb_], ALU.add)))(fb, f),
                waits=[ms_mq[f], ms_rqd[f]], inc="dve")

        if upto >= 2:
            ms_e[2] = load_weight(ew_r[2], 5, waits=[ms_qpe[NDC - 1]])

        bankfree = ({0: ms_rqd[NDC - 2], 1: ms_rqd[NDC - 2]}
                    if upto >= 3 else {})
        ms_dxl = {}
        ms_dj0 = {}
        usage_done = {}
        if upto >= 2:
            usage_done[0] = ms_kpe[NC - 1]
            usage_done[1] = ms_vpe[NC - 1]
            usage_done[2] = ms_qpe[NDC - 1]

        for half in range(2 if upto >= 4 else 0):
            gis = list(range(8 * half, 8 * half + 8))
            jorder = [1, 0] + list(range(2, 9)) if upto >= 5 else [0]
            for j in jorder:
                if j == 0:
                    for il, gi in enumerate(gis):
                        b = gi % 2
                        if half == 0 and il < 2:
                            w = [ms_xn[14 + il], ms_sq[14 + il]]
                        else:
                            w = [ms_dj0[gi - 2]]
                        ms_dxl[gi] = emit(
                            "sync",
                            (lambda g_, b_: (lambda s: s.dma_start(
                                out=x_pp[b_], in_=x_d[P * g_:P * g_ + P, :])))(gi, b),
                            waits=w, inc=f"x{b}")

                        ps = P2a if il % 2 == 0 else P2b

                        def f_j0(t, g_=gi, ps_=ps):
                            t.matmul(ps_[:, 0:512], cb_sb[0:9, P * g_:P * g_ + P],
                                     eb_sb[0:9, 0:512], start=True, stop=False, **MM)
                            t.matmul(ps_[:, 512:1024], cb_sb[0:9, P * g_:P * g_ + P],
                                     eb_sb[0:9, 512:1024], start=True, stop=False, **MM)
                            last = None
                            for d in range(NDC):
                                st = d == NDC - 1
                                t.matmul(ps_[:, 0:512],
                                         qT[:, T * d + P * g_:T * d + P * g_ + P],
                                         M_sb[:, D * d:D * d + 512],
                                         start=False, stop=st, **MM)
                                last = t.matmul(ps_[:, 512:1024],
                                                qT[:, T * d + P * g_:T * d + P * g_ + P],
                                                M_sb[:, D * d + 512:D * d + 1024],
                                                start=False, stop=st, **MM)
                            return last
                        w = [bankfree[il % 2]]
                        if half == 0 and il == 0:
                            w += [ms_qadd[NDC - 1], ms_Mev[NDC - 1], ms_cb]
                        mpe = emit("pe", f_j0, waits=w, inc="pe")

                        def f_acc0(v, g_=gi, il_=il, ps_=ps, b_=b):
                            v.scalar_tensor_tensor(
                                acc[:, D * il_:D * il_ + D], ps_, 1.0,
                                acc[:, D * il_:D * il_ + D], ALU.mult, ALU.add)
                            return v.tensor_tensor(
                                acc[:, D * il_:D * il_ + D],
                                acc[:, D * il_:D * il_ + D], x_pp[b_], ALU.add)
                        ms_dj0[gi] = emit("dve", f_acc0, waits=[mpe, ms_dxl[gi]],
                                          inc="dve")
                        bankfree[il % 2] = ms_dj0[gi]
                    continue

                e = j - 1
                seq_idx = 3 + 8 * half + e
                buf = seq_idx % 3
                last_mpe = None
                for il, gi in enumerate(gis):
                    ps = P2a if il % 2 == 0 else P2b

                    def f_je(t, g_=gi, ps_=ps, bf_=buf):
                        last = None
                        for d in range(NDC):
                            st = d == NDC - 1
                            t.matmul(ps_[:, 0:512],
                                     xnT[:, T * d + P * g_:T * d + P * g_ + P],
                                     W_sb[bf_][:, D * d:D * d + 512],
                                     start=d == 0, stop=st, **MM)
                            last = t.matmul(ps_[:, 512:1024],
                                            xnT[:, T * d + P * g_:T * d + P * g_ + P],
                                            W_sb[bf_][:, D * d + 512:D * d + 1024],
                                            start=d == 0, stop=st, **MM)
                        return last
                    w = [bankfree[il % 2]]
                    if il == 0:
                        w.append(ms_e[seq_idx - 3])
                    last_mpe = emit("pe", f_je, waits=w, inc="pe")

                    if j == 1:
                        def f_acce(v, g_=gi, il_=il, ps_=ps, e_=e):
                            return v.tensor_single_scalar(
                                acc[:, D * il_:D * il_ + D], ps_,
                                c_all[:, 8 * g_ + e_:8 * g_ + e_ + 1], ALU.mult)
                        w2 = [last_mpe]
                        if il == 0:
                            w2.append(ms_rqd[NDC - 1])
                    else:
                        def f_acce(v, g_=gi, il_=il, ps_=ps, e_=e):
                            return v.scalar_tensor_tensor(
                                acc[:, D * il_:D * il_ + D], ps_,
                                c_all[:, 8 * g_ + e_:8 * g_ + e_ + 1],
                                acc[:, D * il_:D * il_ + D], ALU.mult, ALU.add)
                        w2 = [last_mpe]
                    mev = emit("dve", f_acce, waits=w2, inc="dve")
                    bankfree[il % 2] = mev
                    if j == 8:
                        emit("sync",
                             (lambda g_, il_: (lambda s: s.dma_start(
                                 out=y_d[P * g_:P * g_ + P, :],
                                 in_=acc[:, D * il_:D * il_ + D])))(gi, il),
                             waits=[mev], inc="out")
                usage_done[seq_idx] = last_mpe
                nxt = seq_idx + 3
                if nxt <= 18:
                    ne_e = (nxt - 3) % 8
                    ms_e[nxt - 3] = load_weight(
                        ew_r[ne_e], nxt, waits=[usage_done[nxt - 3]])

        if debug:
            emit("sync", lambda s: s.dma_start(out=dbg["xnT"], in_=xnT),
                 waits=[ms_trEvac[ncrun - 1]], inc="out")
            if do_kv:
                emit("sync", lambda s: s.dma_start(out=dbg["vk"], in_=vk_sb),
                     waits=[ms_vk], inc="out")
            if do_gate:
                emit("sync", lambda s: s.dma_start(out=dbg["c"], in_=c_all),
                     waits=[ms_gd2[ncrun - 1]], inc="out")
                emit("sync", lambda s: s.dma_start(out=dbg["cb"], in_=cb_sb),
                     waits=[ms_cb, init_all], inc="out")
            if upto >= 2:
                emit("sync", lambda s: s.dma_start(out=dbg["qT"], in_=qT),
                     waits=[ms_qadd[NDC - 1]], inc="out")
            if upto >= 3:
                emit("sync", lambda s: s.dma_start(out=dbg["M"], in_=M_sb),
                     waits=[ms_Mev[NDC - 1]], inc="out")

        final_out = cnt["out"]
        emit("sync", None, waits=[("out", final_out)])

        block = ent(nc.Block())

        def runner(eng_name):
            def run(eng):
                for waits, fn, minc in ops[eng_name]:
                    for (snm, val) in waits:
                        eng.wait_ge(sems[snm], val)
                    if fn is None:
                        continue
                    inst = fn(eng)
                    if minc is not None:
                        snm, val = minc
                        step = 16 if snm in DMA_SEMS else 1
                        inst.then_inc(sems[snm], step)
            return run

        block.tensor(runner("pe"))
        block.vector(runner("dve"))
        block.scalar(runner("act"))
        block.gpsimd(runner("pool"))
        block.sync(runner("sync"))

    return nc


_NC_CACHE = {}


def get_nc(debug=False, upto=5):
    key = (debug, upto)
    if key not in _NC_CACHE:
        _NC_CACHE[key] = build_nc(debug, upto)
    return _NC_CACHE[key]


def make_in_maps(inputs):
    x = np.ascontiguousarray(np.asarray(inputs["L0_attn_x"], dtype=np.float32))
    common = {
        "wq": np.ascontiguousarray(np.asarray(inputs["L0_attn_Wq"], np.float32)),
        "wk": np.ascontiguousarray(np.asarray(inputs["L0_attn_Wk"], np.float32)),
        "wv": np.ascontiguousarray(np.asarray(inputs["L0_attn_Wv"], np.float32)),
        "wp": np.ascontiguousarray(np.asarray(inputs["L0_attn_Wp"], np.float32)),
        "bq": np.ascontiguousarray(np.asarray(inputs["L0_attn_bq"], np.float32).reshape(1, D)),
        "bk": np.ascontiguousarray(np.asarray(inputs["L0_attn_bk"], np.float32).reshape(1, D)),
        "bv": np.ascontiguousarray(np.asarray(inputs["L0_attn_bv"], np.float32).reshape(1, D)),
        "bp": np.ascontiguousarray(np.asarray(inputs["L0_attn_bp"], np.float32).reshape(1, D)),
        "gw": np.ascontiguousarray(np.asarray(inputs["L0_ffn_gate_w"], np.float32)),
        "smp": None,
        "gb": np.ascontiguousarray(np.asarray(inputs["L0_ffn_gate_b"], np.float32).reshape(1, 10)),
        "ew": np.ascontiguousarray(np.asarray(inputs["L0_ffn_expert_w"], np.float32)),
        "eb": np.ascontiguousarray(np.asarray(inputs["L0_ffn_expert_b"], np.float32)),
    }
    smp = np.zeros((65, D), np.float32)
    smp[0:8] = np.asarray(inputs["L0_ffn_expert_b"], np.float32)
    smp[8] = np.asarray(inputs["L0_attn_bp"], np.float32).reshape(D)
    smp[32] = np.asarray(inputs["L0_attn_bk"], np.float32).reshape(D)
    smp[64] = np.asarray(inputs["L0_attn_bv"], np.float32).reshape(D)
    common["smp"] = smp
    in_maps = []
    for c in range(8):
        m = dict(common)
        m["x"] = np.ascontiguousarray(x[c])
        in_maps.append(m)
    return in_maps


def kernel(**inputs):
    nc = get_nc(False)
    in_maps = make_in_maps(inputs)
    res = run_bass_kernel_spmd(nc, in_maps, core_ids=list(range(8)))
    out = np.stack([np.asarray(r["y"]) for r in res.results], axis=0)
    return out
